# version 05
# baseline (speedup 1.0000x reference)
import contextlib

import numpy as np

import concourse.bass as bass
import concourse.mybir as mybir
from concourse.bass_utils import run_bass_kernel_spmd

F32 = mybir.dt.float32
BF16 = mybir.dt.bfloat16
AF = mybir.ActivationFunctionType
ALU = mybir.AluOpType
AX = mybir.AxisListType

D = 1024
T = 2048
S = 256
E = 8
H = 512
KC = D // 128
CT = S // 128
MT = H // 128
FO = D // 128
NB = T // 512
NCORES = 8
LNEPS = 1e-5
RMSEPS = 1e-8


class Plan:

    def __init__(self):
        self.emit = False
        self.labels = {}
        self.counts = {}
        self.sems = {}
        self.bank_owner = {}

    def inc(self, instr, sem, n=1, label=None):
        c = self.counts.get(sem, 0) + n
        self.counts[sem] = c
        if label is not None:
            if not self.emit:
                assert label not in self.labels, f"dup label {label}"
                self.labels[label] = (sem, c)
            else:
                assert self.labels[label] == (sem, c), (label, self.labels[label], sem, c)
        if self.emit and instr is not None:
            instr.then_inc(self.sems[sem], n)
        return c

    def alias(self, new, old):
        if not self.emit:
            self.labels[new] = self.labels[old]

    def wait(self, eng, label):
        if self.emit:
            sem, v = self.labels[label]
            eng.wait_ge(self.sems[sem], v)

    def bank_wait(self, eng, bank):
        if self.emit:
            tok = self.bank_owner.get(bank)
            if tok is not None:
                self.wait(eng, tok)

    def bank_set(self, bank, label):
        if self.emit:
            self.bank_owner[bank] = label


def build_nc():
    nc = bass.Bass("TRN2", target_bir_lowering=False, debug=False,
                   detect_race_conditions=False)
    P = Plan()

    x_h = nc.dram_tensor("x", [D, T], F32, kind="ExternalInput").ap()
    win_h = nc.dram_tensor("w_in", [D, 768], F32, kind="ExternalInput").ap()
    wx_h = nc.dram_tensor("w_x", [S, S], F32, kind="ExternalInput").ap()
    wdt_h = nc.dram_tensor("w_dt", [S, S], F32, kind="ExternalInput").ap()
    wout_h = nc.dram_tensor("w_out", [S, D], F32, kind="ExternalInput").ap()
    gw_h = nc.dram_tensor("gate_w", [D, E], F32, kind="ExternalInput").ap()
    w1_h = nc.dram_tensor("w1", [E, D, H], F32, kind="ExternalInput").ap()
    w3_h = nc.dram_tensor("w3", [E, D, H], F32, kind="ExternalInput").ap()
    w2_h = nc.dram_tensor("w2fm", [FO, 128, E, MT, 128], F32, kind="ExternalInput").ap()
    bin_h = nc.dram_tensor("b_in_t", [128, 6], F32, kind="ExternalInput").ap()
    bx_h = nc.dram_tensor("b_x_t", [128, CT], F32, kind="ExternalInput").ap()
    bdt_h = nc.dram_tensor("b_dt_t", [128, CT], F32, kind="ExternalInput").ap()
    bout_h = nc.dram_tensor("b_out_t", [128, FO], F32, kind="ExternalInput").ap()
    wnorm_h = nc.dram_tensor("w_norm_t", [128, KC], F32, kind="ExternalInput").ap()
    ident_h = nc.dram_tensor("ident", [128, 128], F32, kind="ExternalInput").ap()
    onehot_h = nc.dram_tensor("onehot", [E, E * 128], F32, kind="ExternalInput").ap()
    ones1_h = nc.dram_tensor("ones_1", [1, 128], F32, kind="ExternalInput").ap()
    onesk_h = nc.dram_tensor("ones_k", [128, 1], F32, kind="ExternalInput").ap()
    eps_h = nc.dram_tensor("eps2", [1, 2], F32, kind="ExternalInput").ap()
    maskp_h = nc.dram_tensor("mask_p", [128, NCORES], F32, kind="ExternalInput").ap()
    maskc_h = nc.dram_tensor("mask_c", [128, NCORES], F32, kind="ExternalInput").ap()
    maskf_h = nc.dram_tensor("mask_f", [128, NCORES], F32, kind="ExternalInput").ap()
    out_h = nc.dram_tensor("out", [D, T], F32, kind="ExternalOutput").ap()
    cc_in = nc.dram_tensor("cc_in", [128, 2 * CT], F32, kind="Internal").ap()
    cc_out = nc.dram_tensor("cc_out", [128 * NCORES, 2 * CT], F32, kind="Internal").ap()

    es = contextlib.ExitStack()
    with es:
        def sba(st, name, shape, dt, side="left"):
            return st.enter_context(nc.sbuf_tensor(name, shape, dt, side=side))

        identb = sba(es, "identb", [128, 128], BF16)
        onehotb = sba(es, "onehotb", [E, E * 128], BF16)
        ones1f = sba(es, "ones1f", [1, 128], F32)
        oneskb = sba(es, "oneskb", [128, 1], BF16)
        binT = sba(es, "binT", [128, 6], F32)
        bxT = sba(es, "bxT", [128, CT], F32)
        bdtT = sba(es, "bdtT", [128, CT], F32)
        bdtN = sba(es, "bdtN", [128, CT], F32)
        boutT = sba(es, "boutT", [128, FO], F32)
        wnormT = sba(es, "wnormT", [128, KC], F32)
        maskP = sba(es, "maskP", [128, NCORES], F32)
        maskC = sba(es, "maskC", [128, NCORES], F32)
        maskF = sba(es, "maskF", [128, NCORES], F32)
        winb = sba(es, "winb", [128, KC * 768], BF16)
        wxb = sba(es, "wxb", [128, CT * S], BF16)
        wdtb = sba(es, "wdtb", [128, CT * S], BF16)
        woutb = sba(es, "woutb", [128, CT * D], BF16)
        gwb = sba(es, "gwb", [128, KC * E], BF16)
        xnb = sba(es, "xnb", [128, KC * T], BF16)
        y_inner = sba(es, "y_inner", [128, CT * T], BF16)
        m_row = sba(es, "m_row", [1, T], F32)
        rstd_row = sba(es, "rstd_row", [1, T], F32)
        srow = sba(es, "srow", [1, T], F32)
        scaleT = sba(es, "scaleT", [E, T], BF16)
        pf = sba(es, "pf", [128, 2 * CT], F32)
        pfall = sba(es, "pfall", [128, NCORES * 2 * CT], F32)
        hh = sba(es, "hh", [128, 2 * CT], F32)
        acar = sba(es, "acar", [128, 1], F32)
        gtmp = sba(es, "gtmp", [128, NCORES], F32)
        gtmp2 = sba(es, "gtmp2", [128, NCORES], F32)
        epsT = sba(es, "epsT", [1, 2], F32)

        for s_ in ["qc", "qx", "qa", "qp", "qd0", "qd1", "qe0", "qe1",
                   "qpf", "qxf0", "qxf1", "qst0", "qst1",
                   "tn", "vn", "sn", "cc"]:
            P.sems[s_] = es.enter_context(nc.semaphore(s_))

        psf = es.enter_context(nc.psum_tensor("psf", [128, 7 * 512], F32))
        psb = es.enter_context(nc.psum_tensor("psb", [128, 1024], BF16))

        def BANK(j, c0=0, n=512, p0=0, np_=128):
            return psf[p0:p0 + np_, j * 512 + c0: j * 512 + c0 + n]

        esA1 = contextlib.ExitStack()
        xb = sba(esA1, "xb", [128, KC * T], BF16)
        t1b = sba(esA1, "t1b", [128, T], BF16)
        mBb = sba(esA1, "mBb", [128, T], BF16)
        rstdBb = sba(esA1, "rstdBb", [128, T], BF16)
        srow2 = sba(esA1, "srow2", [1, T], F32)
        esSq = contextlib.ExitStack()
        sqb = [sba(esSq, f"sqb{i}", [128, T], BF16, side="right") for i in range(2)]
        esA1.close()

        esP1 = contextlib.ExitStack()
        w13b0 = sba(esP1, "w13b0", [128, 2 * KC * H], BF16)
        w2fo0 = sba(esP1, "w2fo0", [128, E * MT * 128], BF16)

        esA2a = contextlib.ExitStack()
        uzT = sba(esA2a, "uzT", [128, 2 * CT * T], BF16)
        evt = sba(esA2a, "evt", [128, 1024], F32)
        g2Bb = sba(esA2a, "g2Bb", [128, T], BF16)
        ut_s = sba(esA2a, "ut_s", [128, 1024], F32)
        dt_s = sba(esA2a, "dt_s", [128, 1024], F32)
        esSq.close()

        esA2b = contextlib.ExitStack()
        a_sb = sba(esA2b, "a_sb", [128, CT * T], F32, side="right")
        b_sb = sba(esA2b, "b_sb", [128, CT * T], F32, side="right")
        sg = sba(esA2b, "sg", [128, CT * T], BF16, side="right")
        esA2a.close()

        esA2c = contextlib.ExitStack()
        hs = sba(esA2c, "hs", [128, CT * T], F32)
        acum = sba(esA2c, "acum", [128, CT * T], BF16)
        logit_sb = sba(esA2c, "logit_sb", [E, T], BF16)
        glog = sba(esA2c, "glog", [128, 128], BF16)
        gd = sba(esA2c, "gd", [128, 128], F32)
        gex = sba(esA2c, "gex", [128, 128], F32)
        gmsk = sba(esA2c, "gmsk", [128, 128], F32)
        gsc = sba(esA2c, "gsc", [128, 128], BF16)
        gm1 = sba(esA2c, "gm1", [128, 16], F32)
        gm2 = sba(esA2c, "gm2", [128, 16], F32)
        gss = sba(esA2c, "gss", [128, 16], F32)
        grs = sba(esA2c, "grs", [128, 16], F32)
        zrow = sba(esA2c, "zrow", [128, 512], F32)
        esA2b.close()
        esA2c.close()

        esB = contextlib.ExitStack()
        w13b1 = sba(esB, "w13b1", [128, 2 * KC * H], BF16)
        w2fo1 = sba(esB, "w2fo1", [128, E * MT * 128], BF16)
        h_blk = sba(esB, "h_blk", [128, E * MT * 512], BF16)
        scBb = [sba(esB, f"scBb{i}", [128, 512], BF16) for i in range(2)]
        sp1b = sba(esB, "sp1b", [128, 1024], BF16)
        silb = sba(esB, "silb", [128, 1024], BF16)
        p3eb = sba(esB, "p3eb", [128, 1024], BF16)
        sp3b = sba(esB, "sp3b", [128, 1024], BF16)
        gamB = sba(esB, "gamB", [128, 512], F32)
        betB = sba(esB, "betB", [128, 512], F32)
        xfx = [sba(esB, f"xfx{i}", [128, 512], F32) for i in range(2)]
        evc2 = sba(esB, "evc2", [128, 512], F32)
        out_t = [sba(esB, f"out_t{i}", [128, 512], F32) for i in range(2)]
        srow_gb = sba(esB, "srow_gb", [1, 1024], F32)
        w13 = [w13b0, w13b1]
        w2fo = [w2fo0, w2fo1]

        block = es.enter_context(nc.Block())

        def xnb_c(k, c0, n=512):
            return xnb[:, k * T + c0: k * T + c0 + n]

        def v3(tile):
            return tile[:].rearrange("p (g e) -> p g e", e=E)

        def col3(t16):
            return t16[:].rearrange("p (g o) -> p g o", o=1).broadcast_to((128, 16, E))

        def emit_gpsimd(g):
            def dma(f, label, sem="qc"):
                i = f() if P.emit else None
                P.inc(i, sem, 16, label)

            dma(lambda: g.dma_start(out=identb[:], in_=ident_h[:]), "c_id")
            dma(lambda: g.dma_start(out=onehotb[:], in_=onehot_h[:]), "c_oh")
            dma(lambda: g.dma_start(out=ones1f[:], in_=ones1_h[:]), "c_o1")
            dma(lambda: g.dma_start(out=oneskb[:], in_=onesk_h[:]), "c_ok")
            dma(lambda: g.dma_start(out=epsT[:], in_=eps_h[:]), "c_eps")
            dma(lambda: g.dma_start(out=binT[:], in_=bin_h[:]), "c_bin")
            dma(lambda: g.dma_start(out=bxT[:], in_=bx_h[:]), "c_bx")
            dma(lambda: g.dma_start(out=bdtT[:], in_=bdt_h[:]), "c_bdt")
            dma(lambda: g.dma_start(out=boutT[:], in_=bout_h[:]), "c_bout")
            dma(lambda: g.dma_start(out=wnormT[:], in_=wnorm_h[:]), "c_wn")
            dma(lambda: g.dma_start(out=maskP[:], in_=maskp_h[:]), "c_mp")
            dma(lambda: g.dma_start(out=maskC[:], in_=maskc_h[:]), "c_mc")
            dma(lambda: g.dma_start(out=maskF[:], in_=maskf_h[:]), "c_mf")
            P.alias("c_all", "c_mf")
            for k in range(KC):
                dma(lambda k=k: g.dma_start(out=xb[:, k * T:(k + 1) * T],
                                            in_=x_h[k * 128:(k + 1) * 128, :]),
                    f"xb{k}", sem="qx")
            dma(lambda: g.dma_start(
                out=winb[:].rearrange("p (k m) -> p k m", k=KC),
                in_=win_h.rearrange("(k p) m -> p k m", p=128)), "w_in", sem="qa")
            dma(lambda: g.dma_start(
                out=wxb[:].rearrange("p (k m) -> p k m", k=CT),
                in_=wx_h.rearrange("(k p) m -> p k m", p=128)), "w_x", sem="qa")
            dma(lambda: g.dma_start(
                out=wdtb[:].rearrange("p (k m) -> p k m", k=CT),
                in_=wdt_h.rearrange("(k p) m -> p k m", p=128)), "w_dt", sem="qa")
            dma(lambda: g.dma_start(
                out=woutb[:].rearrange("p (k m) -> p k m", k=CT),
                in_=wout_h.rearrange("(k p) m -> p k m", p=128)), "w_out", sem="qa")
            dma(lambda: g.dma_start(
                out=gwb[:].rearrange("p (k e) -> p k e", k=KC),
                in_=gw_h.rearrange("(k p) e -> p k e", p=128)), "w_gw", sem="qa")
            P.wait(g, "xnb_done")
            P.wait(g, "sd_done")
            dma(lambda: g.dma_start(
                out=w13b0[:, 0:KC * H].rearrange("p (k m) -> p k m", k=KC),
                in_=w1_h[0].rearrange("(k p) m -> p k m", p=128)), "w13_b0e0_w1", sem="qp")
            dma(lambda: g.dma_start(
                out=w13b0[:, KC * H:].rearrange("p (k m) -> p k m", k=KC),
                in_=w3_h[0].rearrange("(k p) m -> p k m", p=128)), "w13_b0e0_w3", sem="qp")
            dma(lambda: g.dma_start(
                out=w2fo0[:].rearrange("p (e k m) -> p e k m", e=E, k=MT),
                in_=w2_h[0]), "w2_b0f0", sem="qp")
            P.wait(g, "pf_out")
            i = None
            if P.emit:
                i = g.collective_compute(
                    "AllGather", ALU.bypass,
                    replica_groups=[list(range(NCORES))],
                    ins=[cc_in[:]], outs=[cc_out[:]])
            P.inc(i, "cc", 1, "ag_done")
            first_b = [True]

            def b_gate():
                if first_b[0]:
                    P.wait(g, "fixup_done")
                    first_b[0] = False

            for blk in range(NB):
                for e in range(E):
                    if blk == 0 and e == 0:
                        continue
                    idx = blk * E + e
                    buf = idx % 2
                    b_gate()
                    if idx >= 2:
                        pidx = idx - 2
                        P.wait(g, f"p3_b{pidx // E}e{pidx % E}m{MT - 1}")
                    dma(lambda e=e, buf=buf: g.dma_start(
                        out=w13[buf][:, 0:KC * H].rearrange("p (k m) -> p k m", k=KC),
                        in_=w1_h[e].rearrange("(k p) m -> p k m", p=128)),
                        f"w13_b{blk}e{e}_w1", sem=f"qd{idx % 2}")
                    dma(lambda e=e, buf=buf: g.dma_start(
                        out=w13[buf][:, KC * H:].rearrange("p (k m) -> p k m", k=KC),
                        in_=w3_h[e].rearrange("(k p) m -> p k m", p=128)),
                        f"w13_b{blk}e{e}_w3", sem=f"qd{idx % 2}")
                for fo in range(FO):
                    if blk == 0 and fo == 0:
                        continue
                    idx = blk * FO + fo
                    buf = idx % 2
                    b_gate()
                    if idx >= 2:
                        pidx = idx - 2
                        P.wait(g, f"yfo_b{pidx // FO}f{pidx % FO}")
                    dma(lambda fo=fo, buf=buf: g.dma_start(
                        out=w2fo[buf][:].rearrange("p (e k m) -> p e k m", e=E, k=MT),
                        in_=w2_h[fo]), f"w2_b{blk}f{fo}", sem=f"qe{idx % 2}")

        def emit_sync(s):
            def dma(f, label, sem="qpf"):
                i = f() if P.emit else None
                P.inc(i, sem, 16, label)

            P.wait(s, "pf_ready")
            dma(lambda: s.dma_start(out=cc_in[:], in_=pf[:]), "pf_out")
            P.wait(s, "ag_done")
            dma(lambda: s.dma_start(
                out=pfall[:].rearrange("p (r c) -> p r c", r=NCORES),
                in_=cc_out.rearrange("(r p) c -> p r c", p=128)), "pf_in")
            P.wait(s, "fixup_done")

            def xdma(fl):
                blk, fo = fl // FO, fl % FO
                dma(lambda blk=blk, fo=fo, fl=fl: s.dma_start(
                    out=xfx[fl % 2][:],
                    in_=x_h[fo * 128:(fo + 1) * 128, blk * 512:(blk + 1) * 512]),
                    f"xfx_b{blk}f{fo}", sem=f"qxf{fl % 2}")

            xdma(0)
            xdma(1)
            for fl in range(NB * FO):
                blk, fo = fl // FO, fl % FO
                P.wait(s, f"out_b{blk}f{fo}")
                dma(lambda blk=blk, fo=fo, fl=fl: s.dma_start(
                    out=out_h[fo * 128:(fo + 1) * 128, blk * 512:(blk + 1) * 512],
                    in_=out_t[fl % 2][:]), f"st_b{blk}f{fo}", sem=f"qst{fl % 2}")
                if fl + 2 < NB * FO:
                    xdma(fl + 2)
            if P.emit:
                s.wait_ge(P.sems["qst0"], P.counts["qst0"])
                s.wait_ge(P.sems["qst1"], P.counts["qst1"])

        def emit_tensor(t):
            def mmg(bank_ap, pairs, label):
                i = None
                for ii, (lhs, rhs) in enumerate(pairs):
                    if P.emit:
                        i = t.matmul(bank_ap, lhs, rhs,
                                     start=(ii == 0), stop=(ii == len(pairs) - 1))
                P.inc(i, "tn", 1, label)

            P.wait(t, "xb7")
            P.wait(t, "c_all")
            P.wait(t, "c_all")
            P.wait(t, "c_all")
            P.wait(t, "c_all")
            i = None
            for k in range(KC):
                for j in range(NB):
                    if P.emit:
                        i = t.matmul(BANK(j, p0=0, np_=1), oneskb[:, 0:1],
                                     xb[:, k * T + j * 512: k * T + j * 512 + 512],
                                     start=(k == 0), stop=(k == KC - 1))
                P.inc(i, "tn", 1, f"sxu_k{k}")
            for j in range(NB):
                P.wait(t, f"ev_m_j{j}")
            for k in range(KC):
                P.wait(t, f"sq{k}")
                for j in range(NB):
                    if P.emit:
                        i = t.matmul(BANK(j, p0=32, np_=1), oneskb[:, 0:1],
                                     sqb[k % 2][:, j * 512: j * 512 + 512],
                                     start=(k == 0), stop=(k == KC - 1))
                P.inc(i, "tn", 1, f"sq2u_k{k}")
            P.wait(t, "mrow_done")
            for j in range(NB):
                P.wait(t, f"ev_sq_j{j}")
                mmg(BANK(j), [(ones1f[0:1, :], m_row[0:1, j * 512: j * 512 + 512])],
                    f"bc_m_j{j}")
                P.bank_set(j, f"ev_bcm_j{j}")
            for j in range(NB):
                P.bank_wait(t, j)
                mmg(BANK(j), [(ones1f[0:1, :], rstd_row[0:1, j * 512: j * 512 + 512])],
                    f"bc_r_j{j}")
                P.bank_set(j, f"ev_bcr_j{j}")
            for j in range(NB):
                P.bank_wait(t, j)
            for k in range(KC):
                P.wait(t, f"sq2_{k}")
                for j in range(NB):
                    if P.emit:
                        i = t.matmul(BANK(j, p0=64, np_=1), oneskb[:, 0:1],
                                     sqb[k % 2][:, j * 512: j * 512 + 512],
                                     start=(k == 0), stop=(k == KC - 1))
                P.inc(i, "tn", 1, f"s2u_k{k}")
            P.wait(t, "r2row")
            for j in range(NB):
                P.wait(t, f"ev_s2_j{j}")
                mmg(BANK(j), [(ones1f[0:1, :], srow[0:1, j * 512: j * 512 + 512])],
                    f"bc_g2_j{j}")
                P.bank_set(j, f"ev_bcg2_j{j}")
            P.wait(t, "wnorm_fold")
            for m in range(6):
                banks = [0, 1, 2, 3] if m % 2 == 0 else [4, 5, 6, 0]
                for j in range(NB):
                    P.bank_wait(t, banks[j])
                i = None
                for k in range(KC):
                    for j in range(NB):
                        if P.emit:
                            i = t.matmul(
                                BANK(banks[j]),
                                winb[:, (k * 6 + m) * 128:(k * 6 + m) * 128 + 128],
                                xnb_c(k, j * 512),
                                start=(k == 0), stop=(k == KC - 1))
                P.inc(i, "tn", 1, f"uzg_m{m}")
                for j in range(NB):
                    P.bank_set(banks[j], f"uze_m{m}j{j}")
            P.wait(t, "w_gw")
            for ct in range(CT):
                for blk in range(NB):
                    fl = ct * NB + blk
                    bku = fl % 3
                    bkz = 3 + fl % 3
                    P.bank_wait(t, bku)
                    P.wait(t, f"uzg_s_m0j{blk}")
                    P.wait(t, f"uzg_s_m1j{blk}")
                    mmg(BANK(bku),
                        [(wxb[:, (k * CT + ct) * 128:(k * CT + ct) * 128 + 128],
                          uzT[:, k * T + blk * 512: k * T + blk * 512 + 512])
                         for k in range(CT)], f"u2_c{ct}b{blk}")
                    P.bank_set(bku, f"ut_c{ct}b{blk}")
                    P.bank_wait(t, bkz)
                    P.wait(t, f"uzg_s_m2j{blk}")
                    P.wait(t, f"uzg_s_m3j{blk}")
                    mmg(BANK(bkz),
                        [(wdtb[:, (k * CT + ct) * 128:(k * CT + ct) * 128 + 128],
                          uzT[:, (CT + k) * T + blk * 512: (CT + k) * T + blk * 512 + 512])
                         for k in range(CT)], f"z2_c{ct}b{blk}")
                    P.bank_set(bkz, f"a_c{ct}b{blk}")
            P.wait(t, "w_gw")
            for j in range(NB):
                P.bank_wait(t, j)
                mmg(BANK(j, p0=0, np_=E),
                    [(gwb[:, k * E:(k + 1) * E], xnb_c(k, j * 512))
                     for k in range(KC)], f"glog_j{j}")
                P.bank_set(j, f"ev_glog_j{j}")
            for g_ in range(16):
                P.wait(t, f"ev_glog_j{g_ // 4}")
                i = None
                if P.emit:
                    i = t.transpose(psb[:, g_ * 8:(g_ + 1) * 8],
                                    logit_sb[0:E, g_ * 128:(g_ + 1) * 128],
                                    identb[0:E, 0:E])
                P.inc(i, "tn", 1, f"gtr_{g_}")
            for rnd in range(2):
                P.wait(t, "gsc_ready")
                if rnd == 1:
                    P.wait(t, f"ev_gtrb_7")
                for gg in range(8):
                    g_ = rnd * 8 + gg
                    i = None
                    if P.emit:
                        i = t.transpose(psb[0:E, gg * 128:(gg + 1) * 128],
                                        gsc[:, g_ * 8:(g_ + 1) * 8],
                                        identb[:, :])
                    P.inc(i, "tn", 1, f"gtrb_{g_}")
            for blk in range(NB):
                P.wait(t, f"gbrow_b{blk}")
                P.bank_wait(t, 5)
                mmg(BANK(5), [(ones1f[0:1, :], srow_gb[0:1, 0:512])], f"bc_gam_b{blk}")
                P.bank_set(5, f"ev_gam_b{blk}")
                P.bank_wait(t, 6)
                mmg(BANK(6), [(ones1f[0:1, :], srow_gb[0:1, 512:1024])], f"bc_bet_b{blk}")
                P.bank_set(6, f"ev_bet_b{blk}")
                for e in range(E):
                    idx = blk * E + e
                    P.wait(t, "scaleT_done")
                    P.bank_wait(t, 4)
                    mmg(BANK(4),
                        [(onehotb[:, e * 128:(e + 1) * 128],
                          scaleT[0:E, blk * 512:(blk + 1) * 512])], f"scb_b{blk}e{e}")
                    P.bank_set(4, f"ev_scB_b{blk}e{e}")
                    if blk == 0 and e == 0:
                        P.wait(t, "w2_b0f0")
                    else:
                        P.wait(t, f"w13_b{blk}e{e}_w3")
                    for m in range(MT):
                        P.bank_wait(t, m)
                        mmg(BANK(m),
                            [(w13[idx % 2][:, (k * MT + m) * 128:(k * MT + m) * 128 + 128],
                              xnb_c(k, blk * 512)) for k in range(KC)],
                            f"p1_b{blk}e{e}m{m}")
                        P.bank_set(m, f"sp1_b{blk}e{e}m{m}")
                    if blk == 0 and e == 0:
                        P.wait(t, "w2_b0f0")
                    else:
                        P.wait(t, f"w13_b{blk}e{e}_w3")
                    for m in range(MT):
                        P.bank_wait(t, m)
                        mmg(BANK(m),
                            [(w13[idx % 2][:, KC * H + (k * MT + m) * 128:
                                           KC * H + (k * MT + m) * 128 + 128],
                              xnb_c(k, blk * 512)) for k in range(KC)],
                            f"p3_b{blk}e{e}m{m}")
                        P.bank_set(m, f"p3e_b{blk}e{e}m{m}")
                P.wait(t, "w_gw")
                for fo in range(FO):
                    if blk == 0 and fo == 0:
                        P.wait(t, "yinner_done")
                        for e in range(E):
                            P.wait(t, f"h_b{blk}e{e}m{MT - 1}")
                        P.wait(t, "w2_b0f0")
                    else:
                        P.wait(t, f"h_b{blk}e{E - 1}m{MT - 1}")
                        P.wait(t, f"w2_b{blk}f{fo}")
                    bk = 4 + (fo % 3)
                    P.bank_wait(t, bk)
                    idx2 = blk * FO + fo
                    pairs = []
                    for e in range(E):
                        for k in range(MT):
                            pairs.append((
                                w2fo[idx2 % 2][:, (e * MT + k) * 128:(e * MT + k) * 128 + 128],
                                h_blk[:, (e * MT + k) * 512:(e * MT + k) * 512 + 512]))
                    for k2 in range(CT):
                        pairs.append((
                            woutb[:, k2 * D + fo * 128: k2 * D + fo * 128 + 128],
                            y_inner[:, k2 * T + blk * 512: k2 * T + blk * 512 + 512]))
                    mmg(BANK(bk), pairs, f"yfo_b{blk}f{fo}")
                    P.bank_set(bk, f"out_b{blk}f{fo}")

        def emit_scalar(sc):
            def op(f, label=None):
                i = f() if P.emit else None
                if label is not None:
                    P.inc(i, "sn", 1, label)

            for k in range(2):
                P.wait(sc, "xb7")
                op(lambda k=k: sc.activation(sqb[k % 2][:], xb[:, k * T:(k + 1) * T],
                                             AF.Square), f"sq{k}")
            for j in range(NB):
                P.wait(sc, f"sxu_k{KC - 1}")
                op(lambda j=j: sc.activation(m_row[0:1, j * 512:(j + 1) * 512],
                                             BANK(j, p0=0, np_=1), AF.Identity,
                                             scale=1.0 / D), f"ev_m_j{j}")
            for k in range(2, KC):
                P.wait(sc, "xb7")
                P.wait(sc, f"sq2u_k{k - 2}")
                op(lambda k=k: sc.activation(sqb[k % 2][:], xb[:, k * T:(k + 1) * T],
                                             AF.Square), f"sq{k}")
            for j in range(NB):
                P.wait(sc, f"sq2u_k{KC - 1}")
                op(lambda j=j: sc.activation(srow[0:1, j * 512:(j + 1) * 512],
                                             BANK(j, p0=32, np_=1), AF.Identity,
                                             scale=1.0 / D), f"ev_sq_j{j}")
            P.wait(sc, "var_done")
            P.wait(sc, "c_all")
            op(lambda: sc.activation(srow2[0:1, :], srow2[0:1, :], AF.Sqrt,
                                     bias=epsT[0:1, 0:1]), "sd_done")
            for j in range(NB):
                P.wait(sc, f"bc_m_j{j}")
                op(lambda j=j: sc.copy(mBb[:, j * 512:(j + 1) * 512], BANK(j)),
                   f"ev_bcm_j{j}")
            for j in range(NB):
                P.wait(sc, f"bc_r_j{j}")
                op(lambda j=j: sc.copy(rstdBb[:, j * 512:(j + 1) * 512], BANK(j)),
                   f"ev_bcr_j{j}")
            for k in range(KC):
                P.wait(sc, f"xnb{k}")
                if k >= 2:
                    P.wait(sc, f"s2u_k{k - 2}")
                op(lambda k=k: sc.activation(sqb[k % 2][:], xnb[:, k * T:(k + 1) * T],
                                             AF.Square), f"sq2_{k}")
            for j in range(NB):
                P.wait(sc, f"s2u_k{KC - 1}")
                op(lambda j=j: sc.activation(srow[0:1, j * 512:(j + 1) * 512],
                                             BANK(j, p0=64, np_=1), AF.Identity,
                                             scale=1.0 / D), f"ev_s2_j{j}")
            op(lambda: sc.activation(srow[0:1, :], srow[0:1, :], AF.Sqrt,
                                     bias=epsT[0:1, 1:2]), "sd2_done")
            for j in range(NB):
                P.wait(sc, f"bc_g2_j{j}")
                op(lambda j=j: sc.copy(g2Bb[:, j * 512:(j + 1) * 512], BANK(j)),
                   f"ev_bcg2_j{j}")
            P.wait(sc, "c_all")
            for m in range(6):
                for j in range(NB):
                    P.wait(sc, f"uze_v_m{m}j{j}")
                    if m < 4:
                        dst = uzT[:, m * T + j * 512: m * T + j * 512 + 512]
                        fn, bias = AF.Identity, m
                    else:
                        dst = sg[:, (m - 4) * T + j * 512: (m - 4) * T + j * 512 + 512]
                        fn, bias = AF.Sigmoid, m
                    op(lambda dst=dst, fn=fn, bias=bias, j=j: sc.activation(
                        dst, evt[:, (j % 2) * 512:(j % 2) * 512 + 512], fn,
                        bias=binT[:, bias:bias + 1]), f"uzg_s_m{m}j{j}")
            P.wait(sc, "c_all")
            P.wait(sc, "bdtn_done")
            for ct in range(CT):
                for blk in range(NB):
                    fl = ct * NB + blk
                    bku = fl % 3
                    bkz = 3 + fl % 3
                    P.wait(sc, f"u2_c{ct}b{blk}")
                    if fl >= 2:
                        pfl = fl - 2
                        P.wait(sc, f"b_v_c{pfl // NB}b{pfl % NB}")
                    op(lambda blk=blk, bku=bku, ct=ct: sc.activation(
                        ut_s[:, (blk % 2) * 512:(blk % 2) * 512 + 512], BANK(bku),
                        AF.Tanh, bias=bxT[:, ct:ct + 1]), f"ut_c{ct}b{blk}")
                    P.wait(sc, f"z2_c{ct}b{blk}")
                    op(lambda blk=blk, bkz=bkz, ct=ct: sc.activation(
                        dt_s[:, (blk % 2) * 512:(blk % 2) * 512 + 512], BANK(bkz),
                        AF.Sigmoid, bias=bdtT[:, ct:ct + 1]), f"dt_c{ct}b{blk}")
                    op(lambda blk=blk, bkz=bkz, ct=ct: sc.activation(
                        a_sb[:, ct * T + blk * 512: ct * T + blk * 512 + 512],
                        BANK(bkz), AF.Sigmoid, scale=-1.0,
                        bias=bdtN[:, ct:ct + 1]), f"a_c{ct}b{blk}")
            for j in range(NB):
                P.wait(sc, f"glog_j{j}")
                op(lambda j=j: sc.copy(logit_sb[0:E, j * 512:(j + 1) * 512],
                                       BANK(j, p0=0, np_=E)), f"ev_glog_j{j}")
            P.wait(sc, "gd_done")
            op(lambda: sc.activation(gex[:], gd[:], AF.Exp), "gex_done")
            for rnd in range(2):
                P.wait(sc, f"gtrb_{rnd * 8 + 7}")
                for gg in range(8):
                    g_ = rnd * 8 + gg
                    op(lambda g_=g_, gg=gg: sc.copy(
                        scaleT[0:E, g_ * 128:(g_ + 1) * 128],
                        psb[0:E, gg * 128:(gg + 1) * 128]), f"ev_gtrb_{g_}")
            P.alias("scaleT_done", "ev_gtrb_15")
            P.wait(sc, "fixup_done")
            for blk in range(NB):
                if blk >= 1:
                    P.wait(sc, f"out_b{blk - 1}f{FO - 1}")
                P.wait(sc, f"bc_gam_b{blk}")
                op(lambda: sc.copy(gamB[:], BANK(5)), f"ev_gam_b{blk}")
                P.wait(sc, f"bc_bet_b{blk}")
                op(lambda: sc.copy(betB[:], BANK(6)), f"ev_bet_b{blk}")
                for e in range(E):
                    efl = blk * E + e
                    P.wait(sc, f"scb_b{blk}e{e}")
                    if efl >= 2:
                        pe = efl - 2
                        P.wait(sc, f"h_b{pe // E}e{pe % E}m{MT - 1}")
                    op(lambda e=e: sc.copy(scBb[e % 2][:], BANK(4)), f"ev_scB_b{blk}e{e}")
                    for m in range(MT):
                        hfl = efl * MT + m
                        P.wait(sc, f"sp1_v_b{blk}e{e}m{m}")
                        if hfl >= 2:
                            ph = hfl - 2
                            P.wait(sc, f"h_b{ph // (E * MT)}e{(ph // MT) % E}m{ph % MT}")
                        op(lambda m=m: sc.activation(
                            silb[:, (m % 2) * 512:(m % 2) * 512 + 512],
                            sp1b[:, (m % 2) * 512:(m % 2) * 512 + 512],
                            AF.Silu), f"sil_b{blk}e{e}m{m}")
                        P.wait(sc, f"p3_b{blk}e{e}m{m}")
                        op(lambda m=m, bk=m: sc.copy(
                            p3eb[:, (m % 2) * 512:(m % 2) * 512 + 512], BANK(bk)),
                            f"p3e_b{blk}e{e}m{m}")

        def emit_vector(v):
            def op(f, label=None):
                i = f() if P.emit else None
                if label is not None:
                    P.inc(i, "vn", 1, label)

            P.wait(v, "c_all")
            op(lambda: v.tensor_scalar_mul(bdtN[:], bdtT[:], -1.0), "bdtn_done")
            P.wait(v, "c_all")
            P.wait(v, "w_gw")
            for k in range(KC):
                op(lambda k=k: v.tensor_scalar_mul(
                    winb[:, k * 768:(k + 1) * 768], winb[:, k * 768:(k + 1) * 768],
                    wnormT[:, k:k + 1]), f"wnf{k}")
            P.alias("wnorm_fold", f"wnf{KC - 1}")
            P.wait(v, f"ev_m_j{NB - 1}")
            P.wait(v, f"ev_sq_j{NB - 1}")
            op(lambda: v.tensor_mul(srow2[0:1, :], m_row[0:1, :], m_row[0:1, :]))
            op(lambda: v.tensor_sub(srow2[0:1, :], srow[0:1, :], srow2[0:1, :]),
               "var_done")
            P.wait(v, "sd_done")
            op(lambda: v.reciprocal(rstd_row[0:1, :], srow2[0:1, :]), "mrow_done")
            P.wait(v, f"ev_bcm_j{NB - 1}")
            P.wait(v, f"ev_bcr_j{NB - 1}")
            for k in range(KC):
                op(lambda k=k: v.tensor_sub(t1b[:], xb[:, k * T:(k + 1) * T], mBb[:]))
                op(lambda k=k: v.tensor_mul(xnb[:, k * T:(k + 1) * T], t1b[:],
                                            rstdBb[:]), f"xnb{k}")
            P.alias("xnb_done", f"xnb{KC - 1}")
            P.wait(v, "sd2_done")
            op(lambda: v.reciprocal(srow[0:1, :], srow[0:1, :]), "r2row")
            for m in range(6):
                banks = [0, 1, 2, 3] if m % 2 == 0 else [4, 5, 6, 0]
                P.wait(v, f"uzg_m{m}")
                for j in range(NB):
                    fl = m * NB + j
                    if j == 0:
                        P.wait(v, f"ev_bcg2_j{NB - 1}")
                    if fl >= 2:
                        pfl = fl - 2
                        P.wait(v, f"uzg_s_m{pfl // NB}j{pfl % NB}")
                    op(lambda m=m, j=j, bk=banks[j]: v.tensor_mul(
                        evt[:, (j % 2) * 512:(j % 2) * 512 + 512], BANK(bk),
                        g2Bb[:, j * 512:(j + 1) * 512]), f"uze_v_m{m}j{j}")
                    P.alias(f"uze_m{m}j{j}", f"uze_v_m{m}j{j}")
            for ct in range(CT):
                for blk in range(NB):
                    P.wait(v, f"ut_c{ct}b{blk}")
                    P.wait(v, f"dt_c{ct}b{blk}")
                    op(lambda ct=ct, blk=blk: v.tensor_mul(
                        b_sb[:, ct * T + blk * 512: ct * T + blk * 512 + 512],
                        dt_s[:, (blk % 2) * 512:(blk % 2) * 512 + 512],
                        ut_s[:, (blk % 2) * 512:(blk % 2) * 512 + 512]),
                        f"b_v_c{ct}b{blk}")
            P.alias("b_done", f"b_v_c{CT - 1}b{NB - 1}")
            for ct in range(CT):
                P.wait(v, f"a_c{ct}b{NB - 1}")
                op(lambda ct=ct: v.tensor_tensor_scan(
                    hs[:, ct * T:(ct + 1) * T], a_sb[:, ct * T:(ct + 1) * T],
                    b_sb[:, ct * T:(ct + 1) * T], 0.0, ALU.mult, ALU.add),
                    f"hs_c{ct}b{NB - 1}")
            for ct in range(CT):
                op(lambda ct=ct: v.tensor_tensor_scan(
                    acum[:, ct * T:(ct + 1) * T], a_sb[:, ct * T:(ct + 1) * T],
                    a_sb[:, ct * T:(ct + 1) * T], 1.0, ALU.mult, ALU.bypass),
                    f"ac_c{ct}b{NB - 1}")
            for ct in range(CT):
                op(lambda ct=ct: v.tensor_copy(pf[:, ct:ct + 1],
                                               acum[:, ct * T + T - 1: ct * T + T]),
                   f"pp{ct}")
                op(lambda ct=ct: v.tensor_copy(pf[:, 2 + ct:3 + ct],
                                               hs[:, ct * T + T - 1: ct * T + T]),
                   f"pf{ct}")
            P.alias("pf_ready", f"pf{CT - 1}")
            P.wait(v, "gtr_15")
            vch = [0]

            def opc(f, label=None):
                if vch[0]:
                    P.wait(v, f"vch{vch[0]}")
                i = f() if P.emit else None
                vch[0] += 1
                P.inc(i, "vn", 1, f"vch{vch[0]}")
                if label is not None:
                    P.alias(label, f"vch{vch[0]}")
            opc(lambda: v.tensor_copy(glog[:], psb[:, 0:128]), "glog_sb")
            glog3 = v3(glog)
            def o3(t16):
                return t16[:].rearrange("p (g o) -> p g o", o=1)
            opc(lambda: v.reduce_max(o3(gm1), glog3, AX.X))
            opc(lambda: v.tensor_tensor(v3(gd), glog3, col3(gm1), ALU.subtract),
                "gd_done")
            opc(lambda: v.tensor_single_scalar(gmsk[:], gd[:], 0.0, ALU.is_ge))
            opc(lambda: v.tensor_scalar_mul(gmsk[:], gmsk[:], -1e30))
            opc(lambda: v.tensor_add(gmsk[:], gd[:], gmsk[:]))
            gmsk3 = v3(gmsk)
            opc(lambda: v.reduce_max(o3(gm2), gmsk3, AX.X))
            opc(lambda: v.tensor_tensor(gmsk3, v3(gd), col3(gm2), ALU.is_ge))
            P.wait(v, "gex_done")
            gex3 = v3(gex)
            opc(lambda: v.tensor_reduce(o3(gss), gex3, AX.X, ALU.add))
            opc(lambda: v.reciprocal(grs[:], gss[:]))
            opc(lambda: v.tensor_mul(gex[:], gex[:], gmsk[:]))
            opc(lambda: v.tensor_tensor(v3(gsc), gex3, col3(grs), ALU.mult),
                "gsc_ready")
            P.wait(v, "pf_in")
            P.wait(v, "c_all")
            P.wait(v, "c_all")
            P.wait(v, "c_all")
            for ct in range(CT):
                opc(lambda ct=ct: v.memset(hh[:, 2 * ct:2 * ct + 1], 0.0))
                for r in range(NCORES):
                    pslice = pfall[:, r * 2 * CT + ct: r * 2 * CT + ct + 1]
                    fslice = pfall[:, r * 2 * CT + 2 + ct: r * 2 * CT + 2 + ct + 1]
                    opc(lambda pslice=pslice, r=r: v.tensor_mul(
                        gtmp[:, 0:1], pslice, maskP[:, r:r + 1]))
                    opc(lambda r=r: v.tensor_add(gtmp[:, 0:1], gtmp[:, 0:1],
                                                 maskC[:, r:r + 1]))
                    opc(lambda fslice=fslice, r=r: v.tensor_mul(
                        gtmp2[:, 0:1], fslice, maskF[:, r:r + 1]))
                    sr_ = hh[:, 2 * ct + (r % 2): 2 * ct + (r % 2) + 1]
                    ds_ = hh[:, 2 * ct + ((r + 1) % 2): 2 * ct + ((r + 1) % 2) + 1]
                    opc(lambda sr_=sr_, ds_=ds_: v.scalar_tensor_tensor(
                        ds_, sr_, gtmp[:, 0:1], gtmp2[:, 0:1],
                        ALU.mult, ALU.add))
                hfin = hh[:, 2 * ct: 2 * ct + 1]
                opc(lambda ct=ct, hfin=hfin: v.scalar_tensor_tensor(
                    a_sb[:, ct * T:(ct + 1) * T], acum[:, ct * T:(ct + 1) * T],
                    hfin, hs[:, ct * T:(ct + 1) * T], ALU.mult, ALU.add))
                opc(lambda ct=ct: v.tensor_mul(y_inner[:, ct * T:(ct + 1) * T],
                                               a_sb[:, ct * T:(ct + 1) * T],
                                               sg[:, ct * T:(ct + 1) * T]),
                    f"yinner{ct}")
            P.alias("yinner_done", f"yinner{CT - 1}")
            P.alias("fixup_done", f"yinner{CT - 1}")
            P.wait(v, "c_all")
            for blk in range(NB):
                if blk >= 1:
                    P.wait(v, f"bc_bet_b{blk - 1}")
                op(lambda blk=blk: v.tensor_scalar_add(
                    srow_gb[0:1, 0:512], rstd_row[0:1, blk * 512:(blk + 1) * 512],
                    1.0))
                op(lambda blk=blk: v.scalar_tensor_tensor(
                    srow_gb[0:1, 512:1024], m_row[0:1, blk * 512:(blk + 1) * 512],
                    -1.0, rstd_row[0:1, blk * 512:(blk + 1) * 512],
                    ALU.mult, ALU.mult), f"gbrow_b{blk}")
                for e in range(E):
                    efl = blk * E + e
                    for m in range(MT):
                        hfl = efl * MT + m
                        P.wait(v, f"p1_b{blk}e{e}m{m}")
                        P.wait(v, f"ev_scB_b{blk}e{e}")
                        if hfl >= 2:
                            ph = hfl - 2
                            P.wait(v, f"sil_b{ph // (E * MT)}e{(ph // MT) % E}m{ph % MT}")
                        op(lambda m=m, bk=m, e=e: v.tensor_mul(
                            sp1b[:, (m % 2) * 512:(m % 2) * 512 + 512],
                            BANK(bk), scBb[e % 2][:]), f"sp1_v_b{blk}e{e}m{m}")
                        P.alias(f"sp1_b{blk}e{e}m{m}", f"sp1_v_b{blk}e{e}m{m}")
                    for m in range(MT):
                        P.wait(v, f"p3e_b{blk}e{e}m{m}")
                        op(lambda m=m, e=e: v.tensor_mul(
                            sp3b[:, (m % 2) * 512:(m % 2) * 512 + 512],
                            p3eb[:, (m % 2) * 512:(m % 2) * 512 + 512],
                            scBb[e % 2][:]))
                        P.wait(v, f"sil_b{blk}e{e}m{m}")
                        op(lambda e=e, m=m: v.tensor_mul(
                            h_blk[:, (e * MT + m) * 512:(e * MT + m) * 512 + 512],
                            silb[:, (m % 2) * 512:(m % 2) * 512 + 512],
                            sp3b[:, (m % 2) * 512:(m % 2) * 512 + 512]),
                            f"h_b{blk}e{e}m{m}")
                for fo in range(FO):
                    fl = blk * FO + fo
                    bk = 4 + (fo % 3)
                    P.wait(v, f"yfo_b{blk}f{fo}")
                    P.wait(v, f"xfx_b{blk}f{fo}")
                    P.wait(v, f"ev_gam_b{blk}")
                    P.wait(v, f"ev_bet_b{blk}")
                    if fl >= 2:
                        pfl = fl - 2
                        P.wait(v, f"st_b{pfl // FO}f{pfl % FO}")
                    buf = fl % 2
                    op(lambda buf=buf: v.tensor_mul(out_t[buf][:], xfx[buf][:],
                                                    gamB[:]), f"c1_b{blk}f{fo}")
                    op(lambda buf=buf: v.tensor_add(evc2[:], out_t[buf][:], betB[:]))
                    op(lambda buf=buf, fo=fo, bk=bk: v.scalar_tensor_tensor(
                        out_t[buf][:], BANK(bk), boutT[:, fo:fo + 1], evc2[:],
                        ALU.add, ALU.add), f"out_b{blk}f{fo}")

        P.emit = False
        for em in (emit_gpsimd, emit_sync, emit_tensor, emit_scalar, emit_vector):
            P.counts = {}
            em(None)

        P.emit = True
        P.bank_owner = {}

        @block.gpsimd
        def _(g):
            P.counts = {}
            emit_gpsimd(g)

        @block.sync
        def _(s):
            P.counts = {}
            emit_sync(s)

        @block.tensor
        def _(t):
            P.counts = {}
            emit_tensor(t)

        @block.scalar
        def _(sc):
            P.counts = {}
            emit_scalar(sc)

        @block.vector
        def _(v):
            P.counts = {}
            emit_vector(v)

        esB.close()
        esP1.close()

    return nc


_NC_CACHE = None


def _get_nc():
    global _NC_CACHE
    if _NC_CACHE is None:
        _NC_CACHE = build_nc()
    return _NC_CACHE


def kernel(L0_attn_x, L0_attn_w_norm, L0_attn_W_in, L0_attn_b_in, L0_attn_W_x,
           L0_attn_b_x, L0_attn_W_dt, L0_attn_b_dt, L0_attn_W_out, L0_attn_b_out,
           L0_ffn_gate_w, L0_ffn_w1, L0_ffn_w2, L0_ffn_w3):
    x = np.asarray(L0_attn_x, dtype=np.float32)

    def f32(a):
        return np.ascontiguousarray(np.asarray(a, dtype=np.float32))

    w2 = f32(L0_ffn_w2)
    w2fm = np.ascontiguousarray(
        w2.reshape(E, MT, 128, FO, 128).transpose(3, 2, 0, 1, 4))
    onehot = np.zeros((E, E * 128), np.float32)
    for e in range(E):
        onehot[e, e * 128:(e + 1) * 128] = 1.0
    common = {
        "w_in": f32(L0_attn_W_in), "w_x": f32(L0_attn_W_x),
        "w_dt": f32(L0_attn_W_dt), "w_out": f32(L0_attn_W_out),
        "gate_w": f32(L0_ffn_gate_w), "w1": f32(L0_ffn_w1),
        "w3": f32(L0_ffn_w3), "w2fm": w2fm,
        "b_in_t": np.ascontiguousarray(f32(L0_attn_b_in).reshape(6, 128).T),
        "b_x_t": np.ascontiguousarray(f32(L0_attn_b_x).reshape(CT, 128).T),
        "b_dt_t": np.ascontiguousarray(f32(L0_attn_b_dt).reshape(CT, 128).T),
        "b_out_t": np.ascontiguousarray(f32(L0_attn_b_out).reshape(FO, 128).T),
        "w_norm_t": np.ascontiguousarray(f32(L0_attn_w_norm).reshape(KC, 128).T),
        "ident": np.eye(128, dtype=np.float32),
        "onehot": onehot,
        "ones_1": np.ones((1, 128), np.float32),
        "ones_k": np.ones((128, 1), np.float32),
        "eps2": np.array([[LNEPS, RMSEPS]], np.float32),
    }

    in_maps = []
    for core in range(NCORES):
        b, q = core // 4, core % 4
        xT = np.ascontiguousarray(x[b, q * T:(q + 1) * T, :].T)
        mp = np.zeros((128, NCORES), np.float32)
        mf = np.zeros((128, NCORES), np.float32)
        for r in range(NCORES):
            if r // 4 == b and r % 4 < q:
                mp[:, r] = 1.0
                mf[:, r] = 1.0
        mc = np.ascontiguousarray(1.0 - mp)
        m = dict(common)
        m.update({"x": xT, "mask_p": mp, "mask_c": mc, "mask_f": mf})
        in_maps.append(m)

    nc = _get_nc()
    res = run_bass_kernel_spmd(nc, in_maps, core_ids=list(range(NCORES)))
    out = np.empty((2, 8192, D), np.float32)
    for core in range(NCORES):
        b, q = core // 4, core % 4
        out[b, q * T:(q + 1) * T, :] = res.results[core]["out"].T
    return out


if __name__ == "__main__":
    build_nc()
    print("build OK")
